# version 01
# baseline (speedup 1.0000x reference)
import numpy as np
import ml_dtypes
from contextlib import ExitStack

import concourse.bass as bass
import concourse.mybir as mybir
from concourse.alu_op_type import AluOpType
from concourse.bass_utils import run_bass_kernel_spmd

F32 = mybir.dt.float32
BF16 = mybir.dt.bfloat16
AF = mybir.ActivationFunctionType
OP = AluOpType
AXX = mybir.AxisListType.X

P = 128
D = 2048
S = 4096
B = 4
NH = 8
HD = 128
NCH = 16
NST = 8
NQG = 8
W = 8
SCALE = float(1.0 / np.sqrt(128.0))
NST_RUN = NST
NH_RUN = NH
NQG_RUN = NQG
bf16 = ml_dtypes.bfloat16


class Prog:
    def __init__(self):
        self.ops = {e: [] for e in ("tensor", "scalar", "vector", "sync", "gpsimd")}

    def emit(self, eng, fn):
        self.ops[eng].append(fn)

    def install(self, block):
        for eng, ops in self.ops.items():
            if not ops:
                continue
            dec = getattr(block, eng)

            def body(e, ops=ops):
                for fn in ops:
                    fn(e)
            dec(body)


def build_phase_a(nc, es, p, cnt, sems, fused):
    xT = nc.dram_tensor("xT", [D, S], F32, kind="ExternalInput").ap()
    wA = nc.dram_tensor("wA", [D, 3072], F32, kind="ExternalInput").ap()
    onesr_d = nc.dram_tensor("ones_row", [1, P], BF16, kind="ExternalInput").ap()
    onesc_d = nc.dram_tensor("ones_col", [P, 1], BF16, kind="ExternalInput").ap()
    masks_d = nc.dram_tensor("masks", [4, P, 512], BF16, kind="ExternalInput").ap()

    qd = nc.dram_tensor("qd", [1024, S], BF16).ap()
    kd = nc.dram_tensor("kd", [1024, S], BF16).ap()
    vd = nc.dram_tensor("vd", [NH, S, HD], BF16).ap()
    if fused:
        send = nc.dram_tensor("send", [W, NH, P, 512], BF16).ap()
    else:
        send = nc.dram_tensor("send", [W, NH, P, 512], BF16,
                              kind="ExternalOutput").ap()

    sb = lambda name, shape, dt: es.enter_context(nc.sbuf_tensor(name, shape, dt))

    onesr = sb("onesr", [1, P], BF16)
    onesc = sb("onesc", [P, 1], BF16)
    mask_sb = sb("mask_sb", [P, 4, 512], BF16)
    qgm = sb("qgm", [1, NH, NST], F32)
    kgm = sb("kgm", [1, NH, NST], F32)
    kmax = sb("kmax", [1, NH], F32)
    mg8 = sb("mg8", [1, NH, NQG], F32)
    mgb = sb("mgb", [1, NH * NQG], BF16)
    mcol = sb("mcol", [P, NH * NQG], F32)

    es_a1 = ExitStack()
    sb1 = lambda name, shape, dt: es_a1.enter_context(nc.sbuf_tensor(name, shape, dt))
    ps1 = lambda name, shape, dt: es_a1.enter_context(nc.psum_tensor(name, shape, dt))
    wa_sb = sb1("wa_sb", [P, NCH, 3072], BF16)
    xin = [sb1(f"xin{i}", [P, NCH, 512], BF16) for i in range(2)]
    qk_ev = [sb1(f"qk_ev{i}", [P, 512], BF16) for i in range(4)]
    sq_ev = [sb1(f"sq_ev{i}", [P, 512], BF16) for i in range(2)]
    v_ev = [sb1(f"v_ev{i}", [P, 1024], BF16) for i in range(2)]
    pk = [ps1(f"pk{i}", [P, 512], F32) for i in range(2)]
    pv = [ps1(f"pv{i}", [P, 1024], F32) for i in range(2)]
    prow = [ps1(f"prow{i}", [1, 512], F32) for i in range(2)]

    (c_ld, wa_s, x_s0, x_s1, qk0, qk1, qk2, qk3, vt0, vt1, sp0, sp1,
     sn0, sn1, pe_s, ac_s, dv_s) = (sems[k] for k in (
        "c_ld", "wa_s", "x_s0", "x_s1", "qk0", "qk1", "qk2", "qk3",
        "vt0", "vt1", "sp0", "sp1", "sn0", "sn1", "pe_s", "ac_s", "dv_s"))
    x_s = [x_s0, x_s1]
    qk_s = [qk0, qk1, qk2, qk3]
    vt_s = [vt0, vt1]
    sp_s = [sp0, sp1]
    sn_s = [sn0, sn1]

    def f(y):
        y.dma_start(out=onesr[:], in_=onesr_d).then_inc(c_ld, 16)
        y.dma_start(out=onesc[:], in_=onesc_d).then_inc(c_ld, 16)
        y.dma_start(out=mask_sb[:], in_=masks_d.rearrange("m p s -> p m s")).then_inc(c_ld, 16)
    p.emit("sync", f)
    cnt["c_ld"] += 48
    consts_done = cnt["c_ld"]

    def f(g):
        for kk in range(NCH):
            g.dma_start(out=wa_sb[:, kk, :], in_=wA[kk * P:(kk + 1) * P, :]).then_inc(wa_s, 16)
    p.emit("gpsimd", f)
    cnt["wa_s"] += 16 * NCH
    wa_done = cnt["wa_s"]

    def f(v):
        v.memset(qgm[:], 0.0)
        v.memset(kgm[:], 0.0).then_inc(dv_s, 1)
    p.emit("vector", f)
    cnt["dv"] += 1

    x_done = [None] * NST
    pe_x_free = [None] * NST
    ac_evac = []
    st_evac = []
    dv_norm = []
    ac_vev = []
    st_vev = []
    pend_norm = []

    for st in range(NST_RUN):
        def f(g, st=st):
            if st >= 2:
                g.wait_ge(pe_s, pe_x_free[st - 2])
            g.dma_start(out=xin[st % 2][:],
                        in_=xT[:, st * 512:(st + 1) * 512]
                        .rearrange("(c p) s -> p c s", p=P)).then_inc(x_s[st % 2], 16)
        p.emit("gpsimd", f)
        cnt["x_s%d" % (st % 2)] += 16
        x_done[st] = cnt["x_s%d" % (st % 2)]

        for dc in range(16):
            gdc = st * 16 + dc
            is_q = dc < 8
            h = dc % 8
            bank = pk[gdc % 2]

            def f(t, st=st, gdc=gdc, bank=bank,
                  xw=x_done[st], ww=wa_done,
                  aw=(ac_evac[gdc - 2] if gdc >= 2 else None)):
                if gdc == 0:
                    t.wait_ge(wa_s, ww)
                t.wait_ge(x_s[st % 2], xw)
                if aw is not None:
                    t.wait_ge(ac_s, aw)
                for kk in range(NCH):
                    mm = t.matmul(bank[:], wa_sb[:, kk, (gdc % 16) * P:(gdc % 16 + 1) * P],
                                  xin[st % 2][:, kk, :],
                                  start=(kk == 0), stop=(kk == NCH - 1))
                mm.then_inc(pe_s, 1)
            p.emit("tensor", f)
            cnt["pe"] += 1
            pe_dc = cnt["pe"]

            def f(s, gdc=gdc, bank=bank, pe_dc=pe_dc, is_q=is_q,
                  sw=(st_evac[gdc - 4] if gdc >= 4 else None)):
                s.wait_ge(pe_s, pe_dc)
                if sw is not None:
                    s.wait_ge(qk_s[gdc % 4], sw)
                sc = 1.0 if is_q else SCALE
                s.activation(out=qk_ev[gdc % 4][:], in_=bank[:], func=AF.Copy, scale=sc)
                s.activation(out=sq_ev[gdc % 2][:], in_=bank[:], func=AF.Square,
                             scale=sc).then_inc(ac_s, 2)
            p.emit("scalar", f)
            cnt["ac"] += 2
            ac_evac.append(cnt["ac"])

            def f(y, gdc=gdc, is_q=is_q, h=h, st=st, aw=cnt["ac"]):
                y.wait_ge(ac_s, aw)
                tgt = qd if is_q else kd
                y.dma_start(out=tgt[h * P:(h + 1) * P, st * 512:(st + 1) * 512],
                            in_=qk_ev[gdc % 4][:]).then_inc(qk_s[gdc % 4], 16)
            p.emit("sync", f)
            cnt["qk%d" % (gdc % 4)] += 16
            st_evac.append(cnt["qk%d" % (gdc % 4)])

            def norm_emitter(gdc=gdc, is_q=is_q, h=h, st=st, aw=cnt["ac"]):
                def fmm(t, gdc=gdc, aw=aw, cw=consts_done,
                        dw=(dv_norm[gdc - 2] if gdc >= 2 else None)):
                    if gdc == 0:
                        t.wait_ge(c_ld, cw)
                    t.wait_ge(ac_s, aw)
                    if dw is not None:
                        t.wait_ge(dv_s, dw)
                    t.matmul(prow[gdc % 2][:], onesc[:], sq_ev[gdc % 2][:],
                             start=True, stop=True).then_inc(pe_s, 1)
                p.emit("tensor", fmm)
                cnt["pe"] += 1
                pe_norm = cnt["pe"]

                def fcp(v, gdc=gdc, is_q=is_q, h=h, st=st, pe_norm=pe_norm):
                    v.wait_ge(dv_s, 1)
                    v.wait_ge(pe_s, pe_norm)
                    tgt = qgm if is_q else kgm
                    v.tensor_reduce(out=tgt[0:1, h, st:st + 1],
                                    in_=prow[gdc % 2][:], axis=AXX,
                                    op=OP.max).then_inc(dv_s, 1)
                p.emit("vector", fcp)
                cnt["dv"] += 1
                dv_norm.append(cnt["dv"])

            pend_norm.append(norm_emitter)
            if len(pend_norm) >= 2:
                pend_norm.pop(0)()

        for s4 in range(4):
            gs4 = st * 4 + s4

            def f(t, st=st, s4=s4, gs4=gs4, xw=x_done[st],
                  aw=(ac_vev[gs4 - 2] if gs4 >= 2 else None)):
                t.wait_ge(x_s[st % 2], xw)
                if aw is not None:
                    t.wait_ge(ac_s, aw)
                for kk in range(NCH):
                    t.matmul(pv[gs4 % 2][:, 0:512],
                             xin[st % 2][:, kk, s4 * P:(s4 + 1) * P],
                             wa_sb[:, kk, 2048:2560],
                             start=(kk == 0), stop=(kk == NCH - 1))
                    mm = t.matmul(pv[gs4 % 2][:, 512:1024],
                                  xin[st % 2][:, kk, s4 * P:(s4 + 1) * P],
                                  wa_sb[:, kk, 2560:3072],
                                  start=(kk == 0), stop=(kk == NCH - 1))
                mm.then_inc(pe_s, 1)
            p.emit("tensor", f)
            cnt["pe"] += 1
            pe_v = cnt["pe"]

            def f(s, gs4=gs4, pe_v=pe_v,
                  sw=(st_vev[gs4 - 2] if gs4 >= 2 else None)):
                s.wait_ge(pe_s, pe_v)
                if sw is not None:
                    s.wait_ge(vt_s[gs4 % 2], sw)
                s.activation(out=v_ev[gs4 % 2][:], in_=pv[gs4 % 2][:],
                             func=AF.Copy).then_inc(ac_s, 1)
            p.emit("scalar", f)
            cnt["ac"] += 1
            ac_vev.append(cnt["ac"])

            def f(y, st=st, s4=s4, gs4=gs4, aw=cnt["ac"]):
                y.wait_ge(ac_s, aw)
                for hh in range(NH):
                    y.dma_start(out=vd[hh, st * 512 + s4 * P: st * 512 + (s4 + 1) * P, :],
                                in_=v_ev[gs4 % 2][:, hh * HD:(hh + 1) * HD]).then_inc(vt_s[gs4 % 2], 16)
            p.emit("sync", f)
            cnt["vt%d" % (gs4 % 2)] += 16 * NH
            st_vev.append(cnt["vt%d" % (gs4 % 2)])

        pe_x_free[st] = cnt["pe"]

    while pend_norm:
        pend_norm.pop(0)()

    a1_store_totals = [("qk0", cnt["qk0"]), ("qk1", cnt["qk1"]),
                       ("qk2", cnt["qk2"]), ("qk3", cnt["qk3"]),
                       ("vt0", cnt["vt0"]), ("vt1", cnt["vt1"])]
    a1_dv_done = cnt["dv"]
    a1_ac_done = cnt["ac"]

    es_a1.close()
    pa2 = es.enter_context(nc.psum_tensor("pa2", [P, NH * NQG], F32))

    def f(v, dw=a1_dv_done):
        v.wait_ge(dv_s, dw)
        v.tensor_reduce(out=kmax[:], in_=kgm.rearrange("a h s -> a (h s)")[:]
                        .rearrange("a (h s) -> a h s", s=NST),
                        axis=AXX, op=OP.max).then_inc(dv_s, 1)
    p.emit("vector", f)
    cnt["dv"] += 1
    dv_kmax = cnt["dv"]

    def f(s, dw=dv_kmax, a0=a1_ac_done):
        s.wait_ge(dv_s, dw)
        s.activation(out=kmax[:], in_=kmax[:], func=AF.Sqrt).then_inc(ac_s, 1)
        s.wait_ge(ac_s, a0 + 1)
        s.activation(out=qgm[:], in_=qgm[:], func=AF.Sqrt).then_inc(ac_s, 1)
    p.emit("scalar", f)
    cnt["ac"] += 2
    ac_sqrt = cnt["ac"]

    def f(v, aw=ac_sqrt, d0=cnt["dv"]):
        v.wait_ge(ac_s, aw)
        for h in range(NH):
            v.tensor_scalar(out=mg8[0:1, h, :], in0=qgm[0:1, h, :],
                            scalar1=kmax[0:1, h:h + 1], scalar2=-1.0,
                            op0=OP.mult, op1=OP.mult).then_inc(dv_s, 1)
        v.wait_ge(dv_s, d0 + NH)
        v.tensor_copy(out=mgb[:], in_=mg8.rearrange("a h s -> a (h s)")[:]).then_inc(dv_s, 1)
    p.emit("vector", f)
    cnt["dv"] += NH + 1
    dv_mgb = cnt["dv"]

    def f(t, dw=dv_mgb, cw=consts_done):
        t.wait_ge(dv_s, dw)
        t.matmul(pa2[:], onesr[:], mgb[:], start=True, stop=True).then_inc(pe_s, 1)
    p.emit("tensor", f)
    cnt["pe"] += 1
    pe_mc = cnt["pe"]

    def f(v, pw=pe_mc):
        v.wait_ge(pe_s, pw)
        v.tensor_copy(out=mcol[:], in_=pa2[:]).then_inc(dv_s, 1)
    p.emit("vector", f)
    cnt["dv"] += 1
    dv_mcol = cnt["dv"]

    kspan = [sb(f"kspan{i}", [P, S], BF16) for i in range(2)]
    vspan = [sb(f"vspan{i}", [P, 32, HD], BF16) for i in range(2)]
    qspan = [sb(f"qspan{i}", [P, S], BF16) for i in range(2)]
    pt = [sb(f"pt{i}", [P, 1024], BF16) for i in range(2)]
    osb = sb("osb", [P, 512], F32)
    ofin = [sb(f"ofin{i}", [P, 512], BF16) for i in range(2)]
    linvf = sb("linvf", [1, 512], F32)
    linv = sb("linv", [1, 512], BF16)

    st2 = [es.enter_context(nc.psum_tensor(f"st2_{i}", [P, 1024], F32)) for i in range(2)]
    po = es.enter_context(nc.psum_tensor("po", [P, 512], F32))
    plr = es.enter_context(nc.psum_tensor("plr", [1, 512], F32))
    plb = es.enter_context(nc.psum_tensor("plb", [P, 512], F32))

    span_done = [None] * NH
    h_done = [None] * NH
    pair_reuse = [None, None]
    pt_reuse = [None, None]
    lrow_free = None
    po_free = None
    plb_free = None
    send_free = [None, None]

    for h in range(NH_RUN):
        def f(y, h=h, a1w=list(a1_store_totals)):
            if h == 0:
                for nm, tot in a1w:
                    y.wait_ge(sems[nm], tot)
            if h >= 2:
                y.wait_ge(pe_s, h_done[h - 2])
            hb = h % 2
            sc_ = 512 * NST_RUN
            y.dma_start(out=kspan[hb][:, 0:sc_],
                        in_=kd[h * P:(h + 1) * P, 0:sc_]).then_inc(sp_s[hb], 16)
            y.dma_start(out=qspan[hb][:, 0:sc_],
                        in_=qd[h * P:(h + 1) * P, 0:sc_]).then_inc(sp_s[hb], 16)
            y.dma_start(out=vspan[hb][:, 0:4 * NST_RUN, :],
                        in_=vd[h, 0:sc_, :].rearrange("(t p) d -> p t d", p=P)).then_inc(sp_s[hb], 16)
        p.emit("sync", f)
        cnt["sp%d" % (h % 2)] += 48
        span_done[h] = cnt["sp%d" % (h % 2)]

        for i in range(NQG_RUN):
            gidx = h * NQG + i
            nj = 4 * (i + 1)
            npair = nj // 2
            hb = h % 2
            exp_cnt = [None] * npair
            lav_cnt = [None] * nj
            msk_cnt = {}

            def emit_st_pair(pp):
                def f(t, h=h, i=i, pp=pp, hb=hb, gidx=gidx,
                      spw=span_done[h],
                      mcw=(dv_mcol if gidx == 0 else None),
                      prw=pair_reuse[pp % 2]):
                    if pp == 0:
                        t.wait_ge(sp_s[hb], spw)
                        if mcw is not None:
                            t.wait_ge(dv_s, mcw)
                    if prw is not None:
                        t.wait_ge(ac_s, prw)
                    for jh in range(2):
                        j = 2 * pp + jh
                        mm = t.matmul(st2[pp % 2][:, jh * 512:(jh + 1) * 512],
                                      kspan[hb][:, j * P:(j + 1) * P],
                                      qspan[hb][:, i * 512:(i + 1) * 512],
                                      start=True, stop=True)
                    mm.then_inc(pe_s, 1)
                p.emit("tensor", f)
                cnt["pe"] += 1
                mmw = cnt["pe"]
                pair_reuse[pp % 2] = None

                def f(s, pp=pp, gidx=gidx, mmw=mmw, ptw=pt_reuse[pp % 2]):
                    s.wait_ge(pe_s, mmw)
                    if ptw is not None:
                        s.wait_ge(pe_s, ptw)
                    s.activation(out=pt[pp % 2][:], in_=st2[pp % 2][:], func=AF.Exp,
                                 bias=mcol[:, gidx:gidx + 1]).then_inc(ac_s, 1)
                p.emit("scalar", f)
                cnt["ac"] += 1
                exp_cnt[pp] = cnt["ac"]
                pair_reuse[pp % 2] = cnt["ac"]

                for jh in range(2):
                    j = 2 * pp + jh
                    if j >= nj - 4:
                        def f(v, pp=pp, jh=jh, j=j, nj=nj, ew=exp_cnt[pp]):
                            v.wait_ge(ac_s, ew)
                            doff = j - (nj - 4)
                            v.tensor_tensor(out=pt[pp % 2][:, jh * 512:(jh + 1) * 512],
                                            in0=pt[pp % 2][:, jh * 512:(jh + 1) * 512],
                                            in1=mask_sb[:, doff, :],
                                            op=OP.mult).then_inc(dv_s, 1)
                        p.emit("vector", f)
                        cnt["dv"] += 1
                        msk_cnt[j] = cnt["dv"]

            def emit_lav_pair(pp, lrw, pow_, plw):
                for jh in range(2):
                    j = 2 * pp + jh

                    def f(t, h=h, i=i, pp=pp, jh=jh, j=j, hb=hb, nj=nj,
                          ew=exp_cnt[pp], mw=msk_cnt.get(j),
                          lrw=(lrw if j == 0 else None),
                          pow_=(pow_ if j == 0 else None)):
                        t.wait_ge(ac_s, ew)
                        if mw is not None:
                            t.wait_ge(dv_s, mw)
                        if lrw is not None:
                            t.wait_ge(dv_s, lrw)
                        if pow_ is not None:
                            t.wait_ge(ac_s, pow_)
                        sl = pt[pp % 2][:, jh * 512:(jh + 1) * 512]
                        t.matmul(plr[:], onesc[:], sl,
                                 start=(j == 0), stop=(j == nj - 1))
                        t.matmul(po[:], vspan[hb][:, j, :], sl,
                                 start=(j == 0), stop=(j == nj - 1)).then_inc(pe_s, 1)
                    p.emit("tensor", f)
                    cnt["pe"] += 1
                    lav_cnt[j] = cnt["pe"]

            lrw0, pow0, plw0 = lrow_free, po_free, plb_free
            emit_st_pair(0)
            for pp in range(1, npair):
                emit_st_pair(pp)
                emit_lav_pair(pp - 1, lrw0, pow0, plw0)
                pt_reuse[(pp - 1) % 2] = cnt["pe"]
            emit_lav_pair(npair - 1, lrw0, pow0, plw0)
            pt_reuse[(npair - 1) % 2] = cnt["pe"]
            av_done = cnt["pe"]

            def f(v, aw=av_done, d0=cnt["dv"]):
                v.wait_ge(pe_s, aw)
                v.reciprocal(out=linvf[:], in_=plr[:]).then_inc(dv_s, 1)
                v.wait_ge(dv_s, d0 + 1)
                v.tensor_copy(out=linv[:], in_=linvf[:]).then_inc(dv_s, 1)
            p.emit("vector", f)
            cnt["dv"] += 2
            lrow_free = cnt["dv"]

            def f(t, dw=lrow_free, pw=plb_free):
                t.wait_ge(dv_s, dw)
                if pw is not None:
                    t.wait_ge(dv_s, pw)
                t.matmul(plb[:], onesr[:], linv[:], start=True, stop=True).then_inc(pe_s, 1)
            p.emit("tensor", f)
            cnt["pe"] += 1
            pe_lbc = cnt["pe"]

            def f(s, aw=av_done):
                s.wait_ge(pe_s, aw)
                s.activation(out=osb[:], in_=po[:], func=AF.Copy).then_inc(ac_s, 1)
            p.emit("scalar", f)
            cnt["ac"] += 1
            po_free = cnt["ac"]

            def f(v, pw=pe_lbc, aw=po_free, sf=send_free[gidx % 2], fb=gidx % 2):
                v.wait_ge(pe_s, pw)
                v.wait_ge(ac_s, aw)
                if sf is not None:
                    v.wait_ge(sn_s[fb], sf)
                v.tensor_tensor(out=ofin[fb][:], in0=osb[:], in1=plb[:],
                                op=OP.mult).then_inc(dv_s, 1)
            p.emit("vector", f)
            cnt["dv"] += 1
            plb_free = cnt["dv"]
            dv_ofin = cnt["dv"]

            def f(g, h=h, i=i, dw=dv_ofin, fb=gidx % 2):
                g.wait_ge(dv_s, dw)
                g.dma_start(out=send[i, h, :, :], in_=ofin[fb][:]).then_inc(sn_s[fb], 16)
            p.emit("gpsimd", f)
            cnt["sn%d" % (gidx % 2)] += 16
            send_free[gidx % 2] = cnt["sn%d" % (gidx % 2)]

        h_done[h] = cnt["pe"]

    return dict(send=send, send_done=[("sn0", cnt["sn0"]), ("sn1", cnt["sn1"])])


SEM_NAMES_A = ("c_ld", "wa_s", "x_s0", "x_s1", "qk0", "qk1", "qk2", "qk3",
               "vt0", "vt1", "sp0", "sp1", "sn0", "sn1", "pe_s", "ac_s", "dv_s")


def build_a_standalone():
    nc = bass.Bass("TRN2", target_bir_lowering=False)
    p = Prog()
    from collections import defaultdict
    cnt = defaultdict(int)
    es = ExitStack()
    with es:
        sems = {k: es.enter_context(nc.semaphore(k)) for k in SEM_NAMES_A}
        info = build_phase_a(nc, es, p, cnt, sems, fused=False)

        def f(g, done=info["send_done"]):
            for nm, tot in done:
                g.wait_ge(sems[nm], tot)
        p.emit("gpsimd", f)
        block = es.enter_context(nc.Block())
        p.install(block)
    return nc


def prep_phase_a(inputs):
    x = np.asarray(inputs["L0_attn_x"])
    wq = np.asarray(inputs["L0_attn_wq"])
    wk = np.asarray(inputs["L0_attn_wk"])
    wv = np.asarray(inputs["L0_attn_wv"])
    onesr = np.ones((1, P), dtype=bf16)
    onesc = np.ones((P, 1), dtype=bf16)
    masks = np.zeros((4, P, 512), np.float32)
    for d in range(4):
        kkk = np.arange(P)[:, None] + d * P
        qqq = np.arange(512)[None, :]
        masks[d] = (qqq >= kkk).astype(np.float32)
    masks = masks.astype(bf16)
    maps = []
    for c in range(W):
        b, g = c % 4, c // 4
        wsl = lambda w: w[g * 1024:(g + 1) * 1024, :].T
        wAh = np.ascontiguousarray(np.concatenate([wsl(wq), wsl(wk), wsl(wv)], axis=1))
        maps.append({
            "xT": np.ascontiguousarray(x[b].T),
            "wA": wAh,
            "ones_row": onesr, "ones_col": onesc, "masks": masks,
        })
    return maps


def run_phase_a(inputs):
    nc = build_a_standalone()
    maps = prep_phase_a(inputs)
    res = run_bass_kernel_spmd(nc, maps, core_ids=list(range(W)))
    return [np.asarray(r["send"]) for r in res.results]


SEM_NAMES_B = ("r_ld", "wo0", "wo1", "xt0", "xt1", "wf_s", "qe0", "qe1",
               "qb_s", "ud_s", "ut0", "ut1", "ot0", "ot1", "cb_ld")

NPB = 4
NL = 4


def build_phase_b(nc, es, p, cnt, sems, recv_ap, recv_wait=None):
    xs_d = nc.dram_tensor("xs", [NL, D, 512], F32, kind="ExternalInput").ap()
    woT_d = nc.dram_tensor("woT", [D, D], F32, kind="ExternalInput").ap()
    bo_d = nc.dram_tensor("bo_pc", [P, NCH], F32, kind="ExternalInput").ap()
    cw_d = nc.dram_tensor("cw_pc", [P, NCH, 2], F32, kind="ExternalInput").ap()
    wffn_d = [nc.dram_tensor(f"wffn{i}", [D, 1536], F32, kind="ExternalInput").ap()
              for i in range(4)]
    wout_d = nc.dram_tensor("wout", [D, D], F32, kind="ExternalInput").ap()
    rw0_d = nc.dram_tensor("rw0", [1, D], F32, kind="ExternalInput").ap()
    rw1_d = nc.dram_tensor("rw1", [1, D], F32, kind="ExternalInput").ap()
    cos_d = nc.dram_tensor("cosr", [P, NL, 1024], BF16, kind="ExternalInput").ap()
    sin_d = nc.dram_tensor("sinr", [P, NL, 1024], BF16, kind="ExternalInput").ap()
    id_d = nc.dram_tensor("idmat", [P, P], BF16, kind="ExternalInput").ap()
    onesc_d2 = nc.dram_tensor("ones_col_b", [P, 1], BF16, kind="ExternalInput").ap()
    onesrf_d = nc.dram_tensor("ones_row_f", [1, P], F32, kind="ExternalInput").ap()

    out_d = nc.dram_tensor("out_fm", [D, 4 * 512], F32, kind="ExternalOutput").ap()
    qkvg_d = [nc.dram_tensor(f"qkvg{i}", [4 * 512, 1536], BF16).ap() for i in range(4)]
    ud_d = nc.dram_tensor("ud", [D, 4 * 512], BF16).ap()

    (r_ld, wo0, wo1, xt0, xt1, wf_s, qe0, qe1, qb_s, ud_s, ut0, ut1, ot0, ot1,
     cb_ld) = (sems[k] for k in SEM_NAMES_B)
    ot_s = [ot0, ot1]
    pe_s, ac_s, dv_s = sems["pe_s"], sems["ac_s"], sems["dv_s"]
    wo_s = [wo0, wo1]
    xt_s = [xt0, xt1]
    qe_s = [qe0, qe1]
    ut_s = [ut0, ut1]

    sbp = lambda name, shape, dt: es.enter_context(nc.sbuf_tensor(name, shape, dt))

    bo_sb = sbp("bo_sb", [P, NCH], F32)
    cw_sb = sbp("cw_sb", [P, NCH, 2], F32)
    onescb = sbp("onescb", [P, 1], BF16)
    onesrf = sbp("onesrf", [1, P], F32)
    eps1 = sbp("eps1", [1, 1], F32)
    epsc = sbp("epsc", [P, 1], F32)

    def f(y):
        y.dma_start(out=bo_sb[:], in_=bo_d).then_inc(cb_ld, 16)
        y.dma_start(out=cw_sb[:], in_=cw_d).then_inc(cb_ld, 16)
        y.dma_start(out=onescb[:], in_=onesc_d2).then_inc(cb_ld, 16)
        y.dma_start(out=onesrf[:], in_=onesrf_d).then_inc(cb_ld, 16)
    p.emit("sync", f)
    cnt["cb_ld"] += 4 * 16
    cb_done = cnt["cb_ld"]

    def f(g):
        g.memset(eps1[:], 1e-5)
        g.memset(epsc[:], 1e-5).then_inc(cb_ld, 1)
    p.emit("gpsimd", f)
    cnt["cb_ld"] += 1
    cb_done2 = cnt["cb_ld"]


    es_xc = ExitStack()
    xc = es_xc.enter_context(nc.sbuf_tensor("xc", [P, NCH, 4 * 512], BF16))

    es_b1 = ExitStack()
    es_ps1 = ExitStack()
    es_recv = ExitStack()
    sb1 = lambda name, shape, dt: es_b1.enter_context(nc.sbuf_tensor(name, shape, dt))
    ps1 = lambda name, shape, dt: es_ps1.enter_context(nc.psum_tensor(name, shape, dt))
    r1u = sb1("r1u", [P, NCH, 4 * 512], BF16)
    wo_sb = [sb1(f"wo_sb{i}", [P, NCH, P], BF16) for i in range(1)] * 2
    xt_sb = [sb1(f"xt_sb{i}", [P, 512], BF16) for i in range(2)]
    rsq = [sb1(f"rsq{i}", [P, 512], BF16) for i in range(2)]
    rows = sb1("rows", [1, 3, 512], F32)
    pb1 = [ps1(f"pb1_{i}", [P, 512], F32) for i in range(2)]
    pst = [ps1(f"pst{i}", [1, 512], F32) for i in range(2)]
    pmb = ps1("pmb", [P, 512], F32)
    prb = ps1("prb", [P, 512], F32)
    recv_sb = es_recv.enter_context(nc.sbuf_tensor("recv_sb", [P, NCH, 4 * 512], BF16))

    def f(y, rcvw=recv_wait):
        if rcvw is not None:
            y.wait_ge(rcvw[0], rcvw[1])
        for dqc in range(NCH):
            gp, cl = dqc // 8, dqc % 8
            for l in range(NL):
                r = 4 * gp + l
                y.dma_start(out=recv_sb[:, dqc, l * 512:(l + 1) * 512],
                            in_=recv_ap[r, cl, :, :]).then_inc(r_ld, 16)
    p.emit("sync", f)
    cnt["r_ld"] += 16 * 64
    recv_done = cnt["r_ld"]

    dv_evac = {}
    pe_b1 = {}
    for dc in range(NCH):
        def f(g, dc=dc):
            if dc >= 1:
                g.wait_ge(pe_s, pe_b1[(dc - 1, NL - 1)])
            g.dma_start(out=wo_sb[0][:],
                        in_=woT_d[:, dc * P:(dc + 1) * P]
                        .rearrange("(c p) n -> p c n", p=P)).then_inc(wo_s[0], 16)
        p.emit("gpsimd", f)
        cnt["wo0"] += 16
        wo_done = cnt["wo0"]

        for l in range(NL):
            gi = dc * NL + l
            def f(g, dc=dc, l=l, gi=gi):
                if gi >= 2:
                    g.wait_ge(dv_s, dv_evac[gi - 2])
                g.dma_start(out=xt_sb[gi % 2][:],
                            in_=xs_d[l, dc * P:(dc + 1) * P, :]).then_inc(xt_s[gi % 2], 16)
            p.emit("gpsimd", f)
            cnt["xt%d" % (gi % 2)] += 16
            xt_done = cnt["xt%d" % (gi % 2)]

            def f(t, dc=dc, l=l, gi=gi, ww=wo_done, rw=recv_done,
                  aw=(dv_evac.get(gi - 2))):
                if gi == 0:
                    t.wait_ge(r_ld, rw)
                t.wait_ge(wo_s[0], ww)
                if aw is not None:
                    t.wait_ge(dv_s, aw)
                for kk in range(NCH):
                    mm = t.matmul(pb1[gi % 2][:], wo_sb[0][:, kk, :],
                                  recv_sb[:, kk, l * 512:(l + 1) * 512],
                                  start=(kk == 0), stop=(kk == NCH - 1))
                mm.then_inc(pe_s, 1)
            p.emit("tensor", f)
            cnt["pe"] += 1
            pe_b1[(dc, l)] = cnt["pe"]

            def f(v, dc=dc, l=l, gi=gi, pw=cnt["pe"], xw=xt_done, d0=cnt["dv"],
                  cw=cb_done2):
                if gi == 0:
                    v.wait_ge(cb_ld, cw)
                v.wait_ge(pe_s, pw)
                v.wait_ge(xt_s[gi % 2], xw)
                v.wait_ge(dv_s, d0)
                v.scalar_tensor_tensor(out=r1u[:, dc, l * 512:(l + 1) * 512],
                                       in0=pb1[gi % 2][:], scalar=bo_sb[:, dc:dc + 1],
                                       in1=xt_sb[gi % 2][:], op0=OP.add,
                                       op1=OP.add).then_inc(dv_s, 1)
            p.emit("vector", f)
            cnt["dv"] += 1
            dv_evac[gi] = cnt["dv"]

    b1a_dv = cnt["dv"]
    es_recv.close()

    dv_uconv = {}
    pe_stats = {}
    ac_sq = {}
    prev_var_dv = [None]
    prev_norm_dv = [None]
    for l in range(NL):
        for dc in range(NCH):
            def f(s, l=l, dc=dc, dw=b1a_dv,
                  pw=(pe_stats.get((l, dc - 2)) if dc >= 2 else
                      (pe_stats.get((l - 1, NCH - 2 + dc)) if l > 0 else None)),
                  nw=(prev_norm_dv[0] if (dc == 0 and l > 0) else None),
                  sw=(dv_uconv.get((l - 1, NCH - 1)) if (dc <= 1 and l > 0) else None)):
                if dc == 0:
                    s.wait_ge(dv_s, dw)
                if nw is not None:
                    s.wait_ge(dv_s, nw)
                if sw is not None:
                    s.wait_ge(ac_s, sw[1])
                if pw is not None:
                    s.wait_ge(pe_s, pw)
                s.activation(out=rsq[dc % 2][:], in_=r1u[:, dc, l * 512:(l + 1) * 512],
                             func=AF.Square).then_inc(ac_s, 1)
            p.emit("scalar", f)
            cnt["ac"] += 1
            ac_sq[(l, dc)] = cnt["ac"]

            def f(t, l=l, dc=dc, aw=cnt["ac"], dw=b1a_dv, cw=cb_done2,
                  vw=(prev_var_dv[0] if (l > 0 and dc == 0) else None)):
                if l == 0 and dc == 0:
                    t.wait_ge(cb_ld, cw)
                    t.wait_ge(dv_s, dw)
                if vw is not None:
                    t.wait_ge(dv_s, vw)
                t.wait_ge(ac_s, aw)
                t.matmul(pst[0][:], onescb[:], r1u[:, dc, l * 512:(l + 1) * 512],
                         start=(dc == 0), stop=(dc == NCH - 1))
                t.matmul(pst[1][:], onescb[:], rsq[dc % 2][:],
                         start=(dc == 0), stop=(dc == NCH - 1)).then_inc(pe_s, 1)
            p.emit("tensor", f)
            cnt["pe"] += 1
            pe_stats[(l, dc)] = cnt["pe"]

        def f(v, l=l, pw=cnt["pe"], d0=cnt["dv"]):
            v.wait_ge(pe_s, pw)
            v.wait_ge(dv_s, d0)
            v.tensor_scalar(out=rows[0:1, 0, :], in0=pst[0][:], scalar1=-1.0 / D,
                            scalar2=None, op0=OP.mult).then_inc(dv_s, 1)
            v.wait_ge(dv_s, d0 + 1)
            v.tensor_scalar(out=rows[0:1, 1, :], in0=pst[1][:], scalar1=1.0 / D,
                            scalar2=None, op0=OP.mult).then_inc(dv_s, 1)
            v.wait_ge(dv_s, d0 + 2)
            v.tensor_tensor(out=rows[0:1, 2, :], in0=rows[0:1, 0, :],
                            in1=rows[0:1, 0, :], op=OP.mult).then_inc(dv_s, 1)
            v.wait_ge(dv_s, d0 + 3)
            v.tensor_tensor(out=rows[0:1, 1, :], in0=rows[0:1, 1, :],
                            in1=rows[0:1, 2, :], op=OP.subtract).then_inc(dv_s, 1)
        p.emit("vector", f)
        cnt["dv"] += 4
        dv_var = cnt["dv"]

        def f(s, l=l, dw=dv_var, a0=cnt["ac"]):
            s.wait_ge(dv_s, dw)
            s.activation(out=rows[0:1, 2, :], in_=rows[0:1, 1, :], func=AF.Ln,
                         bias=eps1[:]).then_inc(ac_s, 1)
            s.wait_ge(ac_s, a0 + 1)
            s.activation(out=rows[0:1, 2, :], in_=rows[0:1, 2, :], func=AF.Exp,
                         scale=-0.5).then_inc(ac_s, 1)
        p.emit("scalar", f)
        cnt["ac"] += 2
        ac_rstd = cnt["ac"]

        def f(t, l=l, aw=ac_rstd, nw=prev_norm_dv[0]):
            t.wait_ge(ac_s, aw)
            if nw is not None:
                t.wait_ge(dv_s, nw)
            t.matmul(pmb[:], onesrf[:], rows[0:1, 0, :], start=True, stop=True)
            t.matmul(prb[:], onesrf[:], rows[0:1, 2, :], start=True,
                     stop=True).then_inc(pe_s, 1)
        p.emit("tensor", f)
        cnt["pe"] += 1
        pe_bc = cnt["pe"]
        prev_var_dv[0] = dv_var

        for dc in range(NCH):
            def f(v, l=l, dc=dc, pw=pe_bc, d0=cnt["dv"],
                  axw=(dv_uconv.get((l, dc - 1)) or
                       (dv_uconv.get((l - 1, NCH - 1)) if l > 0 else None))):
                v.wait_ge(pe_s, pw)
                if axw is not None:
                    v.wait_ge(ac_s, axw[1])
                v.wait_ge(dv_s, d0)
                sl = r1u[:, dc, l * 512:(l + 1) * 512]
                v.tensor_tensor(out=rsq[0][:], in0=sl, in1=pmb[:],
                                op=OP.add).then_inc(dv_s, 1)
                v.wait_ge(dv_s, d0 + 1)
                v.tensor_tensor(out=sl, in0=rsq[0][:], in1=prb[:],
                                op=OP.mult).then_inc(dv_s, 1)
                v.wait_ge(dv_s, d0 + 2)
                if l == 0:
                    v.tensor_scalar(out=rsq[1][:], in0=sl,
                                    scalar1=cw_sb[:, dc, 1:2], scalar2=None,
                                    op0=OP.mult).then_inc(dv_s, 1)
                else:
                    v.tensor_scalar(out=rsq[0][:], in0=sl,
                                    scalar1=cw_sb[:, dc, 1:2], scalar2=None,
                                    op0=OP.mult).then_inc(dv_s, 1)
                    v.wait_ge(dv_s, d0 + 3)
                    v.scalar_tensor_tensor(out=rsq[1][:],
                                           in0=r1u[:, dc, (l - 1) * 512:l * 512],
                                           scalar=cw_sb[:, dc, 0:1], in1=rsq[0][:],
                                           op0=OP.mult, op1=OP.add).then_inc(dv_s, 1)
            p.emit("vector", f)
            cnt["dv"] += 3 if l == 0 else 4
            dv_conv = cnt["dv"]

            def f(s, l=l, dc=dc, dw=dv_conv):
                s.wait_ge(dv_s, dw)
                s.activation(out=xc[:, dc, l * 512:(l + 1) * 512], in_=rsq[1][:],
                             func=AF.Silu).then_inc(ac_s, 1)
            p.emit("scalar", f)
            cnt["ac"] += 1
            ac_xc = cnt["ac"]

            def f(y, l=l, dc=dc, dw=dv_conv):
                y.wait_ge(dv_s, dw)
                y.dma_start(out=ud_d[dc * P:(dc + 1) * P, l * 512:(l + 1) * 512],
                            in_=r1u[:, dc, l * 512:(l + 1) * 512]).then_inc(ud_s, 16)
            p.emit("sync", f)
            cnt["ud_s"] += 16
            dv_uconv[(l, dc)] = (dv_conv, ac_xc)
        prev_norm_dv[0] = cnt["dv"]

    b1b_ac = cnt["ac"]
    b1b_ud = cnt["ud_s"]
    b1b_dv = cnt["dv"]

    es_ps1.close()
    es_b2 = ExitStack()
    sb2 = lambda name, shape, dt: es_b2.enter_context(nc.sbuf_tensor(name, shape, dt))
    ps2 = lambda name, shape, dt: es_b2.enter_context(nc.psum_tensor(name, shape, dt))
    wf_sb = sb2("wf_sb", [P, NCH, 1536], BF16)
    qe_ev = [sb2(f"qe_ev{i}", [P, 1536], BF16) for i in range(2)]
    b2p = [ps2(f"b2p{i}", [P, 1536], F32) for i in range(2)]

    pe_b2 = {}
    ac_b2 = {}
    qe_done = {}
    for hg in range(4):
        def f(g, hg=hg, pw=(pe_b2.get((hg - 1, 15)) if hg > 0 else pe_b1[(NCH - 1, NL - 1)])):
            g.wait_ge(pe_s, pw)
            for kk in range(NCH):
                g.dma_start(out=wf_sb[:, kk, :],
                            in_=wffn_d[hg][kk * P:(kk + 1) * P, :]).then_inc(wf_s, 16)
        p.emit("gpsimd", f)
        cnt["wf_s"] += 16 * NCH
        wf_done = cnt["wf_s"]

        for tt in range(16):
            def f(t, hg=hg, tt=tt, ww=wf_done,
                  aw=(ac_b2.get((hg, tt - 2)) if tt >= 2 else
                      (ac_b2.get((hg - 1, 14 + tt)) if hg > 0 else None)),
                  xw=b1b_ac):
                if hg == 0 and tt == 0:
                    t.wait_ge(ac_s, xw)
                t.wait_ge(wf_s, ww)
                if aw is not None:
                    t.wait_ge(ac_s, aw)
                for kk in range(NCH):
                    for nn in range(3):
                        mm = t.matmul(b2p[tt % 2][:, nn * 512:(nn + 1) * 512],
                                      xc[:, kk, tt * P:(tt + 1) * P],
                                      wf_sb[:, kk, nn * 512:(nn + 1) * 512],
                                      start=(kk == 0), stop=(kk == NCH - 1))
                mm.then_inc(pe_s, 1)
            p.emit("tensor", f)
            cnt["pe"] += 1
            pe_b2[(hg, tt)] = cnt["pe"]

            def f(s, hg=hg, tt=tt, pw=cnt["pe"],
                  qw=(qe_done.get((hg, tt - 2)) if tt >= 2 else
                      (qe_done.get((hg - 1, 14 + tt)) if hg > 0 else None))):
                s.wait_ge(pe_s, pw)
                if qw is not None:
                    s.wait_ge(qe_s[tt % 2], qw)
                s.activation(out=qe_ev[tt % 2][:, 0:256], in_=b2p[tt % 2][:, 0:256],
                             func=AF.Silu)
                s.activation(out=qe_ev[tt % 2][:, 256:1536],
                             in_=b2p[tt % 2][:, 256:1536],
                             func=AF.Copy).then_inc(ac_s, 2)
            p.emit("scalar", f)
            cnt["ac"] += 2
            ac_b2[(hg, tt)] = cnt["ac"]

            def f(y, hg=hg, tt=tt, aw=cnt["ac"]):
                y.wait_ge(ac_s, aw)
                y.dma_start(out=qkvg_d[hg][tt * P:(tt + 1) * P, :],
                            in_=qe_ev[tt % 2][:]).then_inc(qe_s[tt % 2], 16)
            p.emit("sync", f)
            cnt["qe%d" % (tt % 2)] += 16
            qe_done[(hg, tt)] = cnt["qe%d" % (tt % 2)]

    b2_qe = [("qe0", cnt["qe0"]), ("qe1", cnt["qe1"])]
    es_b2.close()
    es_b1.close()
    es_xc.close()

    es_gated = ExitStack()
    gated_fm = es_gated.enter_context(
        nc.sbuf_tensor("gated_fm", [P, NCH, 4 * 512], BF16))
    es_b3 = ExitStack()
    sb3 = lambda name, shape, dt: es_b3.enter_context(nc.sbuf_tensor(name, shape, dt))
    ps3 = lambda name, shape, dt: es_b3.enter_context(nc.psum_tensor(name, shape, dt))
    cos_sb = sb3("cos_sb", [P, NL, 1024], BF16)
    sin_sb = sb3("sin_sb", [P, NL, 1024], BF16)
    id_sb = sb3("id_sb", [P, P], BF16)
    rwd_bc = sb3("rwd_bc", [P, D], F32)
    es_rw = ExitStack()
    rwrow = es_rw.enter_context(nc.sbuf_tensor("rwrow", [1, 2, D], F32))
    rwdrow = es_rw.enter_context(nc.sbuf_tensor("rwdrow", [1, D], F32))
    prw_b3 = es_rw.enter_context(nc.psum_tensor("prw_b3", [P, 512], F32))

    def f(y, uw=b1b_ud, qw=list(b2_qe), pw=pe_b2[(3, 15)], aw=cnt["ac"],
          dw=cnt["dv"]):
        y.wait_ge(ud_s, uw)
        for nm, tot in qw:
            y.wait_ge(sems[nm], tot)
        y.wait_ge(pe_s, pw)
        y.wait_ge(ac_s, aw)
        y.wait_ge(dv_s, dw)
        y.dma_start(out=cos_sb[:], in_=cos_d).then_inc(cb_ld, 16)
        y.dma_start(out=sin_sb[:], in_=sin_d).then_inc(cb_ld, 16)
        y.dma_start(out=id_sb[:], in_=id_d).then_inc(cb_ld, 16)
        y.dma_start(out=rwrow[0:1, 0, :], in_=rw0_d).then_inc(cb_ld, 16)
        y.dma_start(out=rwrow[0:1, 1, :], in_=rw1_d).then_inc(cb_ld, 16)
    p.emit("sync", f)
    cnt["cb_ld"] += 5 * 16
    b3c_done = cnt["cb_ld"]

    def f(v, cw=b3c_done, d0=cnt["dv"]):
        v.wait_ge(cb_ld, cw)
        v.wait_ge(dv_s, d0)
        v.tensor_tensor(out=rwdrow[:], in0=rwrow[0:1, 0, :],
                        in1=rwrow[0:1, 1, :], op=OP.subtract).then_inc(dv_s, 1)
    p.emit("vector", f)
    cnt["dv"] += 1
    dv_rwd = cnt["dv"]

    d0_rw = cnt["dv"]
    def f(t, dw=dv_rwd, d0=d0_rw):
        t.wait_ge(dv_s, dw)
        for nn in range(4):
            if nn > 0:
                t.wait_ge(dv_s, d0 + nn)
            t.matmul(prw_b3[:], onesrf[:], rwdrow[0:1, nn * 512:(nn + 1) * 512],
                     start=True, stop=True).then_inc(pe_s, 1)
    p.emit("tensor", f)
    p0_rw = cnt["pe"]
    cnt["pe"] += 4

    def f(v, p0=p0_rw, d0=d0_rw):
        for nn in range(4):
            v.wait_ge(pe_s, p0 + nn + 1)
            v.wait_ge(dv_s, d0 + nn)
            v.tensor_copy(out=rwd_bc[:, nn * 512:(nn + 1) * 512],
                          in_=prw_b3[:]).then_inc(dv_s, 1)
    p.emit("vector", f)
    cnt["dv"] += 4
    dv_rwbc = cnt["dv"]
    es_rw.close()

    qb = sb3("qb", [P, NL, 4, 1536], BF16)
    qr = sb3("qr", [P, NL, 1024], BF16)
    kr = sb3("kr", [P, NL, 1024], BF16)
    rt1 = sb3("rt1", [P, 512], BF16)
    rt2 = sb3("rt2", [P, 512], BF16)
    vrw = sb3("vrw", [P, D], F32)
    dlt = sb3("dlt", [P, NL, 16], F32)
    rcf = sb3("rcf", [P, 12, 16], F32)
    qkc = sb3("qkc", [P, 10, 16], F32)
    dots = sb3("dots", [P, 1024], F32)
    o_l = sb3("o_l", [P, 16, HD], F32)
    osq = sb3("osq", [P, 16 * HD], F32)
    hst = sb3("hst", [P, 8, 16], F32)
    sg = sb3("sg", [P, 2048], BF16)
    gt_tmp = sb3("gt_tmp", [P, HD], F32)
    gtm = sb3("gtm", [P, NCH, P], BF16)
    ptb = [ps3(f"ptb{i}", [P, P], BF16) for i in range(2)]

    PAIRS = [(l, m) for l in range(NL) for m in range(l + 1)]
    PIDX = {lm: i for i, lm in enumerate(PAIRS)}

    def dve_seq(v, d0, ops):
        k = d0
        for op in ops:
            v.wait_ge(dv_s, k)
            op(v).then_inc(dv_s, 1)
            k += 1
        return k

    dv_b3 = {}
    ac_b3gate = {}
    pe_tr = {}
    for pb in range(NPB):
        def f(y, pb=pb, qw=list(b2_qe), dw=(dv_b3.get(pb - 1)), uw=b1b_ud,
              rw=dv_rwbc):
            if pb == 0:
                for nm, tot in qw:
                    y.wait_ge(sems[nm], tot)
                y.wait_ge(ud_s, uw)
                y.wait_ge(dv_s, rw)
            if dw is not None:
                y.wait_ge(dv_s, dw)
            for l in range(NL):
                for hg in range(4):
                    y.dma_start(out=qb[:, l, hg, :],
                                in_=qkvg_d[hg][l * 512 + pb * P: l * 512 + (pb + 1) * P, :]
                                ).then_inc(qb_s, 16)
        p.emit("sync", f)
        cnt["qb_s"] += 16 * 16
        qb_done = cnt["qb_s"]

        for l in range(NL):
            for isq in range(2):
                src_off = 0 if isq else 256
                dst = qr if isq == 0 else kr
                def f(v, pb=pb, l=l, isq=isq, qw=qb_done, d0=cnt["dv"], cw=dv_rwbc):
                    if pb == 0 and l == 0 and isq == 0:
                        v.wait_ge(dv_s, cw)
                    v.wait_ge(qb_s, qw)
                    off = 0 if isq == 0 else 256
                    dstt = qr if isq == 0 else kr
                    X1 = qb[:, l, :, off:off + 256].rearrange("p g (h t j) -> p g h t j", t=2, j=32)[:, :, :, 0, :]
                    X2 = qb[:, l, :, off:off + 256].rearrange("p g (h t j) -> p g h t j", t=2, j=32)[:, :, :, 1, :]
                    C1 = cos_sb[:, l, :].rearrange("p (g h t j) -> p g h t j", g=4, t=2, j=32)[:, :, :, 0, :]
                    S1 = sin_sb[:, l, :].rearrange("p (g h t j) -> p g h t j", g=4, t=2, j=32)[:, :, :, 0, :]
                    O1 = dstt[:, l, :].rearrange("p (g h t j) -> p g h t j", g=4, t=2, j=32)[:, :, :, 0, :]
                    O2 = dstt[:, l, :].rearrange("p (g h t j) -> p g h t j", g=4, t=2, j=32)[:, :, :, 1, :]
                    k = d0
                    v.wait_ge(dv_s, k)
                    v.tensor_tensor(out=rt1[:].rearrange("p (g h j) -> p g h j", g=4, j=32)[:],
                                    in0=X1, in1=C1, op=OP.mult).then_inc(dv_s, 1)
                    k += 1
                    v.wait_ge(dv_s, k)
                    v.tensor_tensor(out=rt2[:].rearrange("p (g h j) -> p g h j", g=4, j=32)[:],
                                    in0=X2, in1=S1, op=OP.mult).then_inc(dv_s, 1)
                    k += 1
                    v.wait_ge(dv_s, k)
                    v.tensor_tensor(out=O1,
                                    in0=rt1[:].rearrange("p (g h j) -> p g h j", g=4, j=32)[:],
                                    in1=rt2[:].rearrange("p (g h j) -> p g h j", g=4, j=32)[:],
                                    op=OP.subtract).then_inc(dv_s, 1)
                    k += 1
                    v.wait_ge(dv_s, k)
                    v.tensor_tensor(out=rt1[:].rearrange("p (g h j) -> p g h j", g=4, j=32)[:],
                                    in0=X2, in1=C1, op=OP.mult).then_inc(dv_s, 1)
                    k += 1
                    v.wait_ge(dv_s, k)
                    v.tensor_tensor(out=rt2[:].rearrange("p (g h j) -> p g h j", g=4, j=32)[:],
                                    in0=X1, in1=S1, op=OP.mult).then_inc(dv_s, 1)
                    k += 1
                    v.wait_ge(dv_s, k)
                    v.tensor_tensor(out=O2,
                                    in0=rt1[:].rearrange("p (g h j) -> p g h j", g=4, j=32)[:],
                                    in1=rt2[:].rearrange("p (g h j) -> p g h j", g=4, j=32)[:],
                                    op=OP.add).then_inc(dv_s, 1)
                p.emit("vector", f)
                cnt["dv"] += 6

        for l in range(NL):
            def f(v, pb=pb, l=l, d0=cnt["dv"]):
                VS = qb[:, l, :, 512:1024]
                RW = rwd_bc.rearrange("p (g n) -> p g n", g=4)[:]
                k = d0
                v.wait_ge(dv_s, k)
                v.tensor_tensor(out=vrw.rearrange("p (g n) -> p g n", g=4)[:],
                                in0=VS, in1=RW, op=OP.mult).then_inc(dv_s, 1)
                k += 1
                v.wait_ge(dv_s, k)
                v.tensor_reduce(out=dlt[:, l, :],
                                in_=vrw.rearrange("p (h j) -> p h j", j=P)[:],
                                axis=AXX, op=OP.add).then_inc(dv_s, 1)
            p.emit("vector", f)
            cnt["dv"] += 2

        def f(v, pb=pb, d0=cnt["dv"]):
            k = d0
            for l in range(NL):
                v.wait_ge(dv_s, k)
                v.tensor_scalar(out=rcf[:, l, :], in0=dlt[:, l, :], scalar1=0.0,
                                scalar2=None, op0=OP.is_ge).then_inc(dv_s, 1)
                k += 1
                v.wait_ge(dv_s, k)
                v.tensor_scalar(out=rcf[:, 4 + l, :], in0=dlt[:, l, :], scalar1=-1.0,
                                scalar2=None, op0=OP.mult).then_inc(dv_s, 1)
                k += 1
                v.wait_ge(dv_s, k)
                v.tensor_tensor(out=rcf[:, 4 + l, :], in0=rcf[:, 4 + l, :],
                                in1=dlt[:, l, :], op=OP.max).then_inc(dv_s, 1)
                k += 1
        p.emit("vector", f)
        cnt["dv"] += 12
        dv_c = cnt["dv"]

        def f(s, pb=pb, dw=dv_c, a0=cnt["ac"]):
            s.wait_ge(dv_s, dw)
            s.activation(out=rcf[:, 4:8, :], in_=rcf[:, 4:8, :],
                         func=AF.Sigmoid).then_inc(ac_s, 1)
        p.emit("scalar", f)
        cnt["ac"] += 1
        ac_sm = cnt["ac"]

        def f(v, pb=pb, aw=ac_sm, d0=cnt["dv"]):
            k = d0
            v.wait_ge(ac_s, aw)
            for l in range(NL):
                for m in range(l + 1):
                    pi = PIDX[(l, m)]
                    if m == l:
                        v.wait_ge(dv_s, k)
                        v.memset(qkc[:, pi, :], 1.0).then_inc(dv_s, 1)
                        k += 1
                    else:
                        v.wait_ge(dv_s, k)
                        v.tensor_tensor(out=qkc[:, pi, :], in0=rcf[:, l, :],
                                        in1=rcf[:, m, :], op=OP.subtract).then_inc(dv_s, 1)
                        k += 1
                        v.wait_ge(dv_s, k)
                        v.tensor_tensor(out=qkc[:, pi, :], in0=qkc[:, pi, :],
                                        in1=qkc[:, pi, :], op=OP.mult).then_inc(dv_s, 1)
                        k += 1
                        v.wait_ge(dv_s, k)
                        v.tensor_scalar(out=qkc[:, pi, :], in0=qkc[:, pi, :],
                                        scalar1=-1.0, scalar2=1.0, op0=OP.mult,
                                        op1=OP.add).then_inc(dv_s, 1)
                        k += 1
            for l in range(NL):
                v.wait_ge(dv_s, k)
                v.memset(rcf[:, 8 + l, :], 0.0).then_inc(dv_s, 1)
                k += 1
                for m in range(l + 1):
                    pi = PIDX[(l, m)]
                    v.wait_ge(dv_s, k)
                    v.tensor_tensor(out=rcf[:, 8 + l, :], in0=rcf[:, 8 + l, :],
                                    in1=qkc[:, pi, :], op=OP.add).then_inc(dv_s, 1)
                    k += 1
            for l in range(NL):
                v.wait_ge(dv_s, k)
                v.reciprocal(out=rcf[:, 8 + l, :], in_=rcf[:, 8 + l, :]).then_inc(dv_s, 1)
                k += 1
                v.wait_ge(dv_s, k)
                v.tensor_tensor(out=rcf[:, 4 + l, :], in0=rcf[:, 4 + l, :],
                                in1=rcf[:, 8 + l, :], op=OP.mult).then_inc(dv_s, 1)
                k += 1
                v.wait_ge(dv_s, k)
                v.tensor_scalar(out=rcf[:, 4 + l, :], in0=rcf[:, 4 + l, :],
                                scalar1=0.125, scalar2=None, op0=OP.mult).then_inc(dv_s, 1)
                k += 1
            for l in range(NL):
                for m in range(l + 1):
                    pi = PIDX[(l, m)]
                    v.wait_ge(dv_s, k)
                    v.tensor_tensor(out=qkc[:, pi, :], in0=qkc[:, pi, :],
                                    in1=rcf[:, 4 + l, :], op=OP.mult).then_inc(dv_s, 1)
                    k += 1
        p.emit("vector", f)
        cnt["dv"] += (4 + 3 * 6) + (4 + 10) + 12 + 10
        dv_coef = cnt["dv"]

        for (l, m) in PAIRS:
            pi = PIDX[(l, m)]
            def f(v, pb=pb, l=l, m=m, pi=pi, d0=cnt["dv"]):
                k = d0
                v.wait_ge(dv_s, k)
                v.tensor_tensor(out=dots[:], in0=qr[:, l, :], in1=kr[:, m, :],
                                op=OP.mult).then_inc(dv_s, 1)
                k += 1
                v.wait_ge(dv_s, k)
                v.tensor_reduce(out=rcf[:, 0, :],
                                in_=dots.rearrange("p (h j) -> p h j", j=64)[:],
                                axis=AXX, op=OP.add).then_inc(dv_s, 1)
                k += 1
                v.wait_ge(dv_s, k)
                v.tensor_tensor(out=qkc[:, pi, :], in0=qkc[:, pi, :],
                                in1=rcf[:, 0, :], op=OP.mult).then_inc(dv_s, 1)
            p.emit("vector", f)
            cnt["dv"] += 3
        dv_qkc = cnt["dv"]

        for l in range(NL):
            def f(v, pb=pb, l=l, d0=cnt["dv"]):
                k = d0
                for m in range(l + 1):
                    pi = PIDX[(l, m)]
                    for h in range(16):
                        hg, hh = h // 4, h % 4
                        vs = qb[:, m, hg, 512 + hh * P:512 + (hh + 1) * P]
                        v.wait_ge(dv_s, k)
                        if m == 0:
                            v.tensor_scalar(out=o_l[:, h, :], in0=vs,
                                            scalar1=qkc[:, pi, h:h + 1],
                                            scalar2=None, op0=OP.mult).then_inc(dv_s, 1)
                        else:
                            v.scalar_tensor_tensor(out=o_l[:, h, :], in0=vs,
                                                   scalar=qkc[:, pi, h:h + 1],
                                                   in1=o_l[:, h, :], op0=OP.mult,
                                                   op1=OP.add).then_inc(dv_s, 1)
                        k += 1
            p.emit("vector", f)
            cnt["dv"] += 16 * (l + 1)
            dv_ol = cnt["dv"]

            def f(s, pb=pb, l=l, dw=dv_ol):
                s.wait_ge(dv_s, dw)
                s.activation(out=osq[:], in_=o_l.rearrange("p h j -> p (h j)")[:],
                             func=AF.Square).then_inc(ac_s, 1)
                s.activation(out=sg.rearrange("p (g n) -> p g n", g=4)[:],
                             in_=qb[:, l, :, 1024:1536],
                             func=AF.Silu).then_inc(ac_s, 1)
            p.emit("scalar", f)
            cnt["ac"] += 2
            ac_osq = cnt["ac"]

            def f(v, pb=pb, l=l, aw=ac_osq, d0=cnt["dv"]):
                v.wait_ge(ac_s, aw)
                k = d0
                v.wait_ge(dv_s, k)
                v.tensor_reduce(out=hst[:, 0, :], in_=o_l[:],
                                axis=AXX, op=OP.add).then_inc(dv_s, 1)
                k += 1
                v.wait_ge(dv_s, k)
                v.tensor_reduce(out=hst[:, 1, :],
                                in_=osq.rearrange("p (h j) -> p h j", j=HD)[:],
                                axis=AXX, op=OP.add).then_inc(dv_s, 1)
                k += 1
                v.wait_ge(dv_s, k)
                v.tensor_scalar(out=hst[:, 2, :], in0=hst[:, 0, :],
                                scalar1=-1.0 / HD, scalar2=None,
                                op0=OP.mult).then_inc(dv_s, 1)
                k += 1
                v.wait_ge(dv_s, k)
                v.tensor_scalar(out=hst[:, 3, :], in0=hst[:, 1, :],
                                scalar1=1.0 / HD, scalar2=None,
                                op0=OP.mult).then_inc(dv_s, 1)
                k += 1
                v.wait_ge(dv_s, k)
                v.tensor_tensor(out=hst[:, 4, :], in0=hst[:, 2, :],
                                in1=hst[:, 2, :], op=OP.mult).then_inc(dv_s, 1)
                k += 1
                v.wait_ge(dv_s, k)
                v.tensor_tensor(out=hst[:, 5, :], in0=hst[:, 3, :],
                                in1=hst[:, 4, :], op=OP.subtract).then_inc(dv_s, 1)
            p.emit("vector", f)
            cnt["dv"] += 6
            dv_hvar = cnt["dv"]

            def f(s, pb=pb, l=l, dw=dv_hvar, a0=cnt["ac"]):
                s.wait_ge(dv_s, dw)
                s.activation(out=hst[:, 6, :], in_=hst[:, 5, :], func=AF.Ln,
                             bias=epsc[:]).then_inc(ac_s, 1)
                s.wait_ge(ac_s, a0 + 1)
                s.activation(out=hst[:, 6, :], in_=hst[:, 6, :], func=AF.Exp,
                             scale=-0.5).then_inc(ac_s, 1)
            p.emit("scalar", f)
            cnt["ac"] += 2
            ac_hrstd = cnt["ac"]

            def f(v, pb=pb, l=l, aw=ac_hrstd, d0=cnt["dv"],
                  trw=(pe_tr.get((pb, l - 1)) if l > 0 else
                       (pe_tr.get((pb - 1, NL - 1)) if pb > 0 else None))):
                v.wait_ge(ac_s, aw)
                if trw is not None:
                    v.wait_ge(pe_s, trw)
                k = d0
                for h in range(16):
                    v.wait_ge(dv_s, k)
                    v.scalar_tensor_tensor(out=gt_tmp[:], in0=o_l[:, h, :],
                                           scalar=hst[:, 2, h:h + 1],
                                           in1=sg[:, h * P:(h + 1) * P],
                                           op0=OP.add, op1=OP.mult).then_inc(dv_s, 1)
                    k += 1
                    v.wait_ge(dv_s, k)
                    v.tensor_scalar(out=gtm[:, h, :], in0=gt_tmp[:],
                                    scalar1=hst[:, 6, h:h + 1], scalar2=None,
                                    op0=OP.mult).then_inc(dv_s, 1)
                    k += 1
            p.emit("vector", f)
            cnt["dv"] += 32
            dv_gtm = cnt["dv"]
            dv_b3[pb] = cnt["dv"]

            a_base = cnt["ac"]
            def f(t, pb=pb, l=l, dw=dv_gtm, cw=b3c_done, a_base=a_base):
                t.wait_ge(dv_s, dw)
                if pb == 0 and l == 0:
                    t.wait_ge(cb_ld, cw)
                for ch in range(NCH):
                    if ch >= 2:
                        t.wait_ge(ac_s, a_base + ch - 1)
                    t.transpose(ptb[ch % 2][:], gtm[:, ch, :],
                                id_sb[:]).then_inc(pe_s, 1)
            p.emit("tensor", f)
            p0 = cnt["pe"]
            cnt["pe"] += NCH
            pe_tr[(pb, l)] = cnt["pe"]

            def f(s, pb=pb, l=l, p0=p0):
                for ch in range(NCH):
                    s.wait_ge(pe_s, p0 + ch + 1)
                    s.activation(out=gated_fm[:, ch, l * 512 + pb * P:l * 512 + (pb + 1) * P],
                                 in_=ptb[ch % 2][:], func=AF.Copy).then_inc(ac_s, 1)
            p.emit("scalar", f)
            cnt["ac"] += NCH
            ac_b3gate[(pb, l)] = cnt["ac"]

    b3_ac = cnt["ac"]
    b3_dv = cnt["dv"]
    es_b3.close()

    es_b4 = ExitStack()
    sb4 = lambda name, shape, dt: es_b4.enter_context(nc.sbuf_tensor(name, shape, dt))
    ps4 = lambda name, shape, dt: es_b4.enter_context(nc.psum_tensor(name, shape, dt))
    r2 = sb4("r2", [P, NCH, 4 * 512], BF16)
    wu_sb = [sb4(f"wu_sb{i}", [P, NCH, P], BF16) for i in range(2)]
    ut_sb = [sb4(f"ut_sb{i}", [P, 512], BF16) for i in range(2)]
    rsq4 = [sb4(f"rsq4_{i}", [P, 512], BF16) for i in range(2)]
    rows4 = sb4("rows4", [1, 3, 512], F32)
    otile = [sb4(f"otile{i}", [P, 512], F32) for i in range(2)]
    pb4 = [ps4(f"pb4_{i}", [P, 512], F32) for i in range(2)]
    pst4 = [ps4(f"pst4_{i}", [1, 512], F32) for i in range(2)]
    pmb4 = ps4("pmb4", [P, 512], F32)
    prb4 = ps4("prb4", [P, 512], F32)

    dv_e4 = {}
    pe_b4 = {}
    for dc in range(NCH):
        def f(g, dc=dc, pw=(pe_b4.get((dc - 2, NL - 1))), bw=b3_dv, tw=pe_tr[(NPB - 1, NL - 1)]):
            if dc < 2:
                g.wait_ge(dv_s, bw)
                g.wait_ge(pe_s, tw)
            if pw is not None:
                g.wait_ge(pe_s, pw)
            g.dma_start(out=wu_sb[dc % 2][:],
                        in_=wout_d[:, dc * P:(dc + 1) * P]
                        .rearrange("(c p) n -> p c n", p=P)).then_inc(wo_s[dc % 2], 16)
        p.emit("gpsimd", f)
        cnt["wo%d" % (dc % 2)] += 16
        wu_done = cnt["wo%d" % (dc % 2)]

        for l in range(NL):
            gi = dc * NL + l
            def f(y, dc=dc, l=l, gi=gi, uw=b1b_ud, dw=(dv_e4.get(gi - 2)),
                  bw=b3_dv, aw=b3_ac):
                if gi == 0:
                    y.wait_ge(ud_s, uw)
                    y.wait_ge(dv_s, bw)
                    y.wait_ge(ac_s, aw)
                if dw is not None:
                    y.wait_ge(dv_s, dw)
                y.dma_start(out=ut_sb[gi % 2][:],
                            in_=ud_d[dc * P:(dc + 1) * P, l * 512:(l + 1) * 512]
                            ).then_inc(ut_s[gi % 2], 16)
            p.emit("sync", f)
            cnt["ut%d" % (gi % 2)] += 16
            ut_done = cnt["ut%d" % (gi % 2)]

            def f(t, dc=dc, l=l, gi=gi, ww=wu_done, aw=b3_ac,
                  ew=(dv_e4.get(gi - 2))):
                if gi == 0:
                    t.wait_ge(ac_s, aw)
                t.wait_ge(wo_s[dc % 2], ww)
                if ew is not None:
                    t.wait_ge(dv_s, ew)
                for kk in range(NCH):
                    mm = t.matmul(pb4[gi % 2][:], wu_sb[dc % 2][:, kk, :],
                                  gated_fm[:, kk, l * 512:(l + 1) * 512],
                                  start=(kk == 0), stop=(kk == NCH - 1))
                mm.then_inc(pe_s, 1)
            p.emit("tensor", f)
            cnt["pe"] += 1
            pe_b4[(dc, l)] = cnt["pe"]

            def f(v, dc=dc, l=l, gi=gi, pw=cnt["pe"], uw=ut_done, d0=cnt["dv"]):
                v.wait_ge(pe_s, pw)
                v.wait_ge(ut_s[gi % 2], uw)
                v.wait_ge(dv_s, d0)
                v.tensor_tensor(out=r2[:, dc, l * 512:(l + 1) * 512],
                                in0=pb4[gi % 2][:], in1=ut_sb[gi % 2][:],
                                op=OP.add).then_inc(dv_s, 1)
            p.emit("vector", f)
            cnt["dv"] += 1
            dv_e4[gi] = cnt["dv"]

    b4a_dv = cnt["dv"]

    ac_sq4 = {}
    pe_st4 = {}
    ot_watch = {}
    prev_var4 = [None]
    prev_norm4 = [None]
    for l in range(NL):
        for dc in range(NCH):
            def f(s, l=l, dc=dc, dw=b4a_dv,
                  pw=(pe_st4.get((l, dc - 2)) if dc >= 2 else
                      (pe_st4.get((l - 1, NCH - 2 + dc)) if l > 0 else None))):
                if dc == 0:
                    s.wait_ge(dv_s, dw)
                if pw is not None:
                    s.wait_ge(pe_s, pw)
                s.activation(out=rsq4[dc % 2][:], in_=r2[:, dc, l * 512:(l + 1) * 512],
                             func=AF.Square).then_inc(ac_s, 1)
            p.emit("scalar", f)
            cnt["ac"] += 1

            def f(t, l=l, dc=dc, aw=cnt["ac"], dw=b4a_dv,
                  vw=(prev_var4[0] if (l > 0 and dc == 0) else None)):
                if l == 0 and dc == 0:
                    t.wait_ge(dv_s, dw)
                if vw is not None:
                    t.wait_ge(dv_s, vw)
                t.wait_ge(ac_s, aw)
                t.matmul(pst4[0][:], onescb[:], r2[:, dc, l * 512:(l + 1) * 512],
                         start=(dc == 0), stop=(dc == NCH - 1))
                t.matmul(pst4[1][:], onescb[:], rsq4[dc % 2][:],
                         start=(dc == 0), stop=(dc == NCH - 1)).then_inc(pe_s, 1)
            p.emit("tensor", f)
            cnt["pe"] += 1
            pe_st4[(l, dc)] = cnt["pe"]

        def f(v, l=l, pw=cnt["pe"], d0=cnt["dv"]):
            v.wait_ge(pe_s, pw)
            v.wait_ge(dv_s, d0)
            v.tensor_scalar(out=rows4[0:1, 0, :], in0=pst4[0][:], scalar1=-1.0 / D,
                            scalar2=None, op0=OP.mult).then_inc(dv_s, 1)
            v.wait_ge(dv_s, d0 + 1)
            v.tensor_scalar(out=rows4[0:1, 1, :], in0=pst4[1][:], scalar1=1.0 / D,
                            scalar2=None, op0=OP.mult).then_inc(dv_s, 1)
            v.wait_ge(dv_s, d0 + 2)
            v.tensor_tensor(out=rows4[0:1, 2, :], in0=rows4[0:1, 0, :],
                            in1=rows4[0:1, 0, :], op=OP.mult).then_inc(dv_s, 1)
            v.wait_ge(dv_s, d0 + 3)
            v.tensor_tensor(out=rows4[0:1, 1, :], in0=rows4[0:1, 1, :],
                            in1=rows4[0:1, 2, :], op=OP.subtract).then_inc(dv_s, 1)
        p.emit("vector", f)
        cnt["dv"] += 4
        dv_var4 = cnt["dv"]

        def f(s, l=l, dw=dv_var4, a0=cnt["ac"]):
            s.wait_ge(dv_s, dw)
            s.activation(out=rows4[0:1, 2, :], in_=rows4[0:1, 1, :], func=AF.Ln,
                         bias=eps1[:]).then_inc(ac_s, 1)
            s.wait_ge(ac_s, a0 + 1)
            s.activation(out=rows4[0:1, 2, :], in_=rows4[0:1, 2, :], func=AF.Exp,
                         scale=-0.5).then_inc(ac_s, 1)
        p.emit("scalar", f)
        cnt["ac"] += 2
        ac_rstd4 = cnt["ac"]

        def f(t, l=l, aw=ac_rstd4, nw=prev_norm4[0]):
            t.wait_ge(ac_s, aw)
            if nw is not None:
                t.wait_ge(dv_s, nw)
            t.matmul(pmb4[:], onesrf[:], rows4[0:1, 0, :], start=True, stop=True)
            t.matmul(prb4[:], onesrf[:], rows4[0:1, 2, :], start=True,
                     stop=True).then_inc(pe_s, 1)
        p.emit("tensor", f)
        cnt["pe"] += 1
        pe_bc4 = cnt["pe"]
        prev_var4[0] = dv_var4

        for dc in range(NCH):
            gi = l * NCH + dc
            def f(v, l=l, dc=dc, gi=gi, pw=pe_bc4, d0=cnt["dv"],
                  ow=(ot_watch.get(gi - 2))):
                v.wait_ge(pe_s, pw)
                v.wait_ge(dv_s, d0)
                if ow is not None:
                    v.wait_ge(ot_s[gi % 2], ow)
                sl = r2[:, dc, l * 512:(l + 1) * 512]
                v.tensor_tensor(out=otile[gi % 2][:], in0=sl, in1=pmb4[:],
                                op=OP.add).then_inc(dv_s, 1)
                v.wait_ge(dv_s, d0 + 1)
                v.tensor_tensor(out=otile[gi % 2][:], in0=otile[gi % 2][:],
                                in1=prb4[:], op=OP.mult).then_inc(dv_s, 1)
            p.emit("vector", f)
            cnt["dv"] += 2
            dv_ot = cnt["dv"]

            def f(y, l=l, dc=dc, gi=gi, dw=dv_ot):
                y.wait_ge(dv_s, dw)
                y.dma_start(out=out_d[dc * P:(dc + 1) * P, l * 512:(l + 1) * 512],
                            in_=otile[gi % 2][:]).then_inc(ot_s[gi % 2], 16)
            p.emit("sync", f)
            cnt["ot%d" % (gi % 2)] += 16
            ot_watch[gi] = cnt["ot%d" % (gi % 2)]
        prev_norm4[0] = cnt["dv"]

    es_b4.close()
    es_gated.close()
    return dict(out=out_d, out_done=[("ot0", cnt["ot0"]), ("ot1", cnt["ot1"])])


def build_b_standalone():
    nc = bass.Bass("TRN2", target_bir_lowering=False)
    p = Prog()
    from collections import defaultdict
    cnt = defaultdict(int)
    es = ExitStack()
    with es:
        sems = {k: es.enter_context(nc.semaphore(k))
                for k in dict.fromkeys(SEM_NAMES_B + ("pe_s", "ac_s", "dv_s"))}
        recv = nc.dram_tensor("recv", [W, NH, P, 512], BF16,
                              kind="ExternalInput").ap()
        info = build_phase_b(nc, es, p, cnt, sems, recv)

        def f(y, done=info["out_done"]):
            for nm, tot in done:
                y.wait_ge(sems[nm], tot)
        p.emit("sync", f)
        block = es.enter_context(nc.Block())
        p.install(block)
    return nc


def prep_phase_b(inputs):
    x = np.asarray(inputs["L0_attn_x"])
    wo = np.asarray(inputs["L0_attn_wo"])
    bo = np.asarray(inputs["L0_attn_bo"])
    cw = np.asarray(inputs["L0_ffn_conv_w"])
    Wq = np.asarray(inputs["L0_ffn_Wq"])
    Wk = np.asarray(inputs["L0_ffn_Wk"])
    Wv = np.asarray(inputs["L0_ffn_Wv"])
    Wg = np.asarray(inputs["L0_ffn_Wg"])
    rw = np.asarray(inputs["L0_ffn_router_w"])
    Wout = np.asarray(inputs["L0_ffn_Wout"])

    woT = np.ascontiguousarray(wo.T)
    bo_pc = np.ascontiguousarray(bo.reshape(NCH, P).T)
    cw_pc = np.ascontiguousarray(cw.reshape(NCH, P, 2).transpose(1, 0, 2))
    wffn = [np.ascontiguousarray(np.concatenate(
        [Wq[:, hg * 256:(hg + 1) * 256], Wk[:, hg * 256:(hg + 1) * 256],
         Wv[:, hg * 512:(hg + 1) * 512], Wg[:, hg * 512:(hg + 1) * 512]], axis=1))
        for hg in range(4)]
    rw0 = np.ascontiguousarray(rw[:, :, 0].reshape(1, D))
    rw1 = np.ascontiguousarray(rw[:, :, 1].reshape(1, D))

    invf = 1.0 / (10000.0 ** (np.arange(0, 64, 2, dtype=np.float64) / 64.0))
    ang = np.arange(NL)[:, None] * invf[None, :]
    off = np.arange(64) % 32
    cos1 = np.cos(ang)[:, off]
    sin1 = np.sin(ang)[:, off]
    cosr = np.broadcast_to(np.tile(cos1, (1, 16)).astype(bf16)[None, :, :],
                           (P, NL, 1024)).copy()
    sinr = np.broadcast_to(np.tile(sin1, (1, 16)).astype(bf16)[None, :, :],
                           (P, NL, 1024)).copy()
    idmat = np.eye(P, dtype=bf16)
    onescb = np.ones((P, 1), dtype=bf16)
    onesrf = np.ones((1, P), np.float32)

    shared = {"woT": woT, "bo_pc": bo_pc, "cw_pc": cw_pc,
              "wffn0": wffn[0], "wffn1": wffn[1], "wffn2": wffn[2],
              "wffn3": wffn[3], "wout": np.ascontiguousarray(Wout),
              "rw0": rw0, "rw1": rw1, "cosr": cosr, "sinr": sinr,
              "idmat": idmat, "ones_col_b": onescb, "ones_row_f": onesrf}
    maps = []
    for c in range(W):
        m = dict(shared)
        m["xs"] = np.ascontiguousarray(
            x[:, c * 512:(c + 1) * 512, :].transpose(0, 2, 1))
        maps.append(m)
    return maps


def assemble_output(outs):
    out = np.empty((B, S, D), np.float32)
    for c in range(W):
        of = np.asarray(outs[c], np.float32)
        out[:, c * 512:(c + 1) * 512, :] = of.reshape(D, NL, 512).transpose(1, 2, 0)
    return out


def kernel(**inputs):
    nc_a = build_a_standalone()
    maps_a = prep_phase_a(inputs)
    res_a = run_bass_kernel_spmd(nc_a, maps_a, core_ids=list(range(W)))
    sends = [np.asarray(r["send"]) for r in res_a.results]

    nc_b = build_b_standalone()
    maps_b = prep_phase_b(inputs)
    for c in range(W):
        maps_b[c]["recv"] = np.stack([sends[r][c] for r in range(W)])
    res_b = run_bass_kernel_spmd(nc_b, maps_b, core_ids=list(range(W)))
    return assemble_output([r["out_fm"] for r in res_b.results])
